# version 02
# baseline (speedup 1.0000x reference)
import numpy as np
import ml_dtypes
import concourse.bass as bass
import concourse.mybir as mybir
from concourse.bass_utils import run_bass_kernel_spmd

F32 = mybir.dt.float32
BF16 = mybir.dt.bfloat16
I32 = mybir.dt.int32
AF = mybir.ActivationFunctionType
ALU = mybir.AluOpType
AX = mybir.AxisListType

TOK = 2048
PAD = 2176
D = 2048
DI = 8192
EPS = 1e-5
C2048 = float(np.float32(1.0 / np.sqrt(2048.0)))
C8192 = float(np.float32(1.0 / np.sqrt(8192.0)))
NEGM = -1.0e30
INV127 = float(np.float32(1.0 / 127.0))

ENGS = ("te", "ve", "se", "ge", "sy")
ESEM = dict(te="ste", ve="sve", se="sse", ge="sge", sy="ssy")

PFC = 31360
P_WA, P_WB, P_G = 0, 8192, 16384
A_XL, A_XN, A_XT, A_WA, A_WB, A_KS, A_VS, A_JK = (0, 2048, 4096, 12288,
                                                  13312, 14336, 18432, 22528)
A_QW, A_KW, A_VW, A_OS = 0, 4096, 10240, 16384
A_P0, A_P1, A_PN0, A_PN1, A_PT0, A_PT1 = 20480, 20736, 20992, 21248, 21504, 21760
B_OW, B_WA, B_WB, B_Y, B_XL, B_JK = 0, 8192, 10240, 12288, 20480, 22528
F_OVL, F_X2, F_PS, F_Y, F_V = 0, 8192, 10240, 26624, 28672
F_VB1, F_VB2, F_ZTA, F_ZTB = 30720, 30976, 31104, 31232
F_ZG1 = F_OVL + 4096

PB_T = 0
F_K1, F_K2 = 0, 4096
F_WG, F_WG2, F_WU, F_WU2, F_TD, F_TD2 = 20480, 21504, 22528, 23552, 24576, 26624
PBC = 28672


class Rec:
    def __init__(self):
        self.ops = {e: [] for e in ENGS}
        self.cnt = dict(ste=0, sve=0, sse=0, sge=0, ssy=0, dsy=0, dse=0, dge=0)
        self.hwm = {}

    def _w(self, eng, waits):
        out = []
        for t in waits:
            if t is None:
                continue
            s, v, pe = t
            if pe == eng:
                continue
            k = (eng, s)
            if self.hwm.get(k, -1) >= v:
                continue
            self.hwm[k] = v
            out.append((s, v))
        return out

    def op(self, eng, meth, *args, waits=(), inc=True, **kw):
        ws = self._w(eng, waits)
        sem = ESEM[eng]
        if inc:
            self.cnt[sem] += 1
        self.ops[eng].append((meth, args, kw, ws, (sem, 1) if inc else None))
        return (sem, self.cnt[sem], eng)

    def dma(self, eng, q, out, in_, waits=()):
        ws = self._w(eng, waits)
        self.cnt[q] += 16
        self.ops[eng].append(("dma_start", (), dict(out=out, in_=in_), ws, (q, 16)))
        return (q, self.cnt[q], None)

    def wait_only(self, eng, waits):
        ws = self._w(eng, waits)
        if ws:
            self.ops[eng].append((None, (), {}, ws, None))

    def barrier(self):
        toks = [(s, self.cnt[s], e) for e, s in ESEM.items()]
        toks += [("dsy", self.cnt["dsy"], None), ("dse", self.cnt["dse"], None)]
        for e in ENGS:
            self.wait_only(e, toks)

    def now(self, eng):
        return (ESEM[eng], self.cnt[ESEM[eng]], eng)


def build():
    nc = bass.Bass("TRN2", target_bir_lowering=False)

    x_h = nc.dram_tensor("x", [PAD, D], F32, kind="ExternalInput")
    wq_h = nc.dram_tensor("wq", [D, D], F32, kind="ExternalInput")
    wk_h = nc.dram_tensor("wk", [D, D], F32, kind="ExternalInput")
    wv_h = nc.dram_tensor("wv", [D, D], F32, kind="ExternalInput")
    wo_h = nc.dram_tensor("wo", [D, D], F32, kind="ExternalInput")
    wg_h = nc.dram_tensor("wg", [D, DI], F32, kind="ExternalInput")
    wu_h = nc.dram_tensor("wu", [D, DI], F32, kind="ExternalInput")
    wd_h = nc.dram_tensor("wd", [DI, D], F32, kind="ExternalInput")
    ch_h = nc.dram_tensor("ch128", [128, 128], F32, kind="ExternalInput")
    cw16_h = nc.dram_tensor("cw16", [128, 128], F32, kind="ExternalInput")
    cw64_h = nc.dram_tensor("cw64", [128, 128], F32, kind="ExternalInput")
    cid_h = nc.dram_tensor("cid", [128, 128], F32, kind="ExternalInput")
    cidb_h = nc.dram_tensor("cidb", [128, 128], BF16, kind="ExternalInput")
    cmask_h = nc.dram_tensor("cmask", [128, 512], BF16, kind="ExternalInput")
    csel_h = nc.dram_tensor("csel", [8, 1024], F32, kind="ExternalInput")
    y_h = nc.dram_tensor("y", [TOK, D], F32, kind="ExternalOutput")

    import os as _os
    _ik = "ExternalOutput" if _os.environ.get("KDBG") else "Internal"
    kd_h = nc.dram_tensor("kd", [D, PAD], F32, kind=_ik)
    vd_h = nc.dram_tensor("vd", [PAD, D], F32, kind=_ik)
    qd_h = nc.dram_tensor("qd", [D, PAD], F32, kind=_ik)
    od_h = nc.dram_tensor("od", [D, TOK], F32, kind=_ik)
    xm_h = nc.dram_tensor("xm", [TOK, D], F32, kind=_ik)
    tg_h = nc.dram_tensor("tg", [D, DI], BF16, kind=_ik)
    tu_h = nc.dram_tensor("tu", [D, DI], BF16, kind=_ik)
    td_h = nc.dram_tensor("td", [DI, D], BF16, kind=_ik)
    DBG = bool(_os.environ.get("KDBG"))
    if DBG:
        dd1_h = nc.dram_tensor("dd1", [128, 2048], F32, kind="ExternalOutput")
        dd2_h = nc.dram_tensor("dd2", [128, 8192], F32, kind="ExternalOutput")
        dd3_h = nc.dram_tensor("dd3", [128, 64], F32, kind="ExternalOutput")

    pf = nc.alloc_sbuf_tensor("pf", [128, PFC], F32)
    pb = nc.alloc_sbuf_tensor("pbf", [128, PBC], BF16)
    i32t = nc.alloc_sbuf_tensor("i32t", [128, 2048], I32)
    cH = nc.alloc_sbuf_tensor("cH", [128, 128], F32)
    cW16 = nc.alloc_sbuf_tensor("cW16", [128, 128], F32)
    cW64 = nc.alloc_sbuf_tensor("cW64", [128, 128], F32)
    cId = nc.alloc_sbuf_tensor("cId", [128, 128], F32)
    cIdb = nc.alloc_sbuf_tensor("cIdb", [128, 128], BF16)
    cMask = nc.alloc_sbuf_tensor("cMask", [128, 512], BF16)
    cSel = nc.alloc_sbuf_tensor("cSel", [8, 1024], F32)
    ones1 = nc.alloc_sbuf_tensor("ones1", [128, 1], F32)
    stt = nc.alloc_sbuf_tensor("stt", [128, 64], F32)
    srow = nc.alloc_sbuf_tensor("srow", [8, 128], F32)
    sc11 = nc.alloc_sbuf_tensor("sc11", [1, 1], F32)
    ps = [nc.alloc_psum_tensor("ps%d" % i, [128, 512], F32) for i in range(8)]

    r = Rec()
    bank_tok = [None] * 8

    def pv(c0, n):
        return pf[:, c0:c0 + n]

    def pap(off, dims):
        return bass.AP(pf, off, [[PFC, 128]] + dims)

    def bap(off, dims):
        return bass.AP(pb, off, [[PBC, 128]] + dims)

    def psap(b, dims):
        return bass.AP(ps[b], 0, [[512, 128]] + dims)

    def copy_evac(eng, dst, src, waits):
        if eng == "ve":
            return r.op("ve", "tensor_copy", dst, src, waits=waits)
        return r.op("se", "activation", dst, src, AF.Copy, waits=waits)

    for dst, src in ((cH, ch_h), (cW16, cw16_h), (cW64, cw64_h), (cId, cid_h),
                     (cIdb, cidb_h), (cMask, cmask_h), (cSel, csel_h)):
        r.dma("sy", "dsy", dst[:], src.ap())
    r.op("ge", "memset", ones1[:], 1.0)
    r.barrier()

    def ln_stats(xap, jk_off, tok_in, extra=()):
        t1 = r.op("ve", "reduce_sum", stt[:, 26:27], xap, AX.X,
                  waits=[tok_in] + list(extra))
        t2 = r.op("se", "activation", pv(jk_off, D), xap, AF.Square,
                  accum_out=stt[:, 27:28], waits=[tok_in] + list(extra))
        r.op("ve", "tensor_single_scalar", stt[:, 28:29], stt[:, 26:27],
             1.0 / D, ALU.mult, waits=[t1])
        r.op("ve", "tensor_mul", stt[:, 29:30], stt[:, 28:29], stt[:, 28:29])
        tv = r.op("ve", "scalar_tensor_tensor", stt[:, 30:31], stt[:, 27:28],
                  1.0 / D, stt[:, 29:30], ALU.mult, ALU.subtract, waits=[t2])
        tv = r.op("ve", "tensor_single_scalar", stt[:, 30:31], stt[:, 30:31],
                  EPS, ALU.add, waits=[tv])
        tsd = r.op("se", "activation", stt[:, 31:32], stt[:, 30:31], AF.Sqrt,
                   waits=[tv])
        ti = r.op("ve", "reciprocal", stt[:, 24:25], stt[:, 31:32], waits=[tsd])
        tn = r.op("ve", "scalar_tensor_tensor", stt[:, 25:26], stt[:, 28:29],
                  -1.0, stt[:, 24:25], ALU.mult, ALU.mult, waits=[ti])
        return tn

    def ln_apply(out_ap, xap, tok_stats, extra=()):
        return r.op("ve", "tensor_scalar", out_ap, xap,
                    stt[:, 24:25], stt[:, 25:26], ALU.mult, ALU.add,
                    waits=[tok_stats] + list(extra))

    def transpose_chunks(dst_off, dst_cols, colbase, src_off, tok_in, war=()):
        toks = []
        for c in range(16):
            b = 4 + (c % 4)
            tt = r.op("te", "transpose", ps[b][:, 0:128],
                      pv(src_off + c * 128, 128), cId[:],
                      waits=[tok_in, bank_tok[b]] + list(war))
            ev = copy_evac("ve" if c % 2 == 0 else "se",
                           pap(dst_off + c * dst_cols + colbase, [[1, 128]]),
                           ps[b][:, 0:128], [tt])
            bank_tok[b] = ev
            toks.append(ev)
        return toks

    plast = [None, None]
    for (w_h, t_h, ncols, ntiles, sc) in ((wg_h, tg_h, DI, 16, 0),
                                          (wu_h, tu_h, DI, 16, 8),
                                          (wd_h, td_h, D, 64, 16)):
        nelem = float(ncols) * 128 * ntiles
        last = plast
        acc = None
        prev_acc = None
        for i in range(ntiles):
            buf = P_WA if (i % 2 == 0) else P_WB
            dt = r.dma("sy", "dsy", pv(buf, ncols),
                       bass.AP(w_h, i * 128 * ncols, [[ncols, 128], [1, ncols]]),
                       waits=[last[i % 2]])
            ct = r.op("se", "activation", pv(P_G, ncols), pv(buf, ncols),
                      AF.Abs, accum_out=stt[:, 32:33],
                      waits=[dt, prev_acc])
            if i == 0:
                acc = r.op("ve", "tensor_copy", stt[:, 33:34], stt[:, 32:33],
                           waits=[ct])
            else:
                acc = r.op("ve", "tensor_add", stt[:, 33:34], stt[:, 33:34],
                           stt[:, 32:33], waits=[ct])
            prev_acc = acc
            last[i % 2] = acc
        t = r.op("te", "matmul", ps[0][0:1, 0:1], stt[:, 33:34], ones1[:],
                 start=True, stop=True, waits=[acc, bank_tok[0]])
        t = r.op("ve", "tensor_copy", sc11[:], ps[0][0:1, 0:1], waits=[t])
        t2 = r.op("te", "matmul", ps[0][:, 0:1], cSel[0:1, 896:1024], sc11[:],
                  start=True, stop=True, waits=[t])
        tmean = r.op("se", "activation", stt[:, 34:35], ps[0][:, 0:1], AF.Copy,
                     scale=1.0 / nelem, waits=[t2])
        bank_tok[0] = tmean
        ts_ = r.op("ve", "tensor_single_scalar", stt[:, sc:sc + 1],
                   stt[:, 34:35], EPS, ALU.max, waits=[tmean])
        th = r.op("ve", "tensor_single_scalar", stt[:, sc + 1:sc + 2],
                  stt[:, sc:sc + 1], 0.5, ALU.mult, waits=[ts_])
        tnh = r.op("ve", "tensor_single_scalar", stt[:, sc + 2:sc + 3],
                   stt[:, sc + 1:sc + 2], -1.0, ALU.mult, waits=[th])
        last = [last[0], last[1]]
        lastst = [None, None]
        for i in range(ntiles):
            buf = P_WA if (i % 2 == 0) else P_WB
            tbo = 0 if (i % 2 == 0) else 8192
            dt = r.dma("sy", "dsy", pv(buf, ncols),
                       bass.AP(w_h, i * 128 * ncols, [[ncols, 128], [1, ncols]]),
                       waits=[last[i % 2]])
            tg_ = r.op("ve", "tensor_single_scalar", pv(P_G, ncols),
                       pv(buf, ncols), stt[:, sc + 1:sc + 2], ALU.is_gt,
                       waits=[dt, th])
            tt_ = r.op("ve", "scalar_tensor_tensor", pb[:, tbo:tbo + ncols],
                       pv(buf, ncols), stt[:, sc + 2:sc + 3], pv(P_G, ncols),
                       ALU.is_lt, ALU.subtract, waits=[tnh, lastst[i % 2]])
            st = r.dma("se", "dse",
                       bass.AP(t_h, i * 128 * ncols, [[ncols, 128], [1, ncols]]),
                       pb[:, tbo:tbo + ncols], waits=[tt_])
            last[i % 2] = st
            lastst[i % 2] = st
        plast = last
    r.barrier()

    ttr = []
    prev_apply = None
    stage_tok = [None] * 8
    vst = [None, None]
    prev15 = None
    for (row0, nt) in ((0, 512), (512, 512), (1024, 512), (1536, 512),
                       (2048, 128)):
        nsub = nt // 128
        ttr_blk = []
        for s in range(nsub):
            tin = r.dma("se", "dse", pv(A_XL, D),
                        bass.AP(x_h, (row0 + s * 128) * D, [[D, 128], [1, D]]),
                        waits=ttr + [r.now("te"), r.now("ve")])
            tst = ln_stats(pv(A_XL, D), A_JK, tin, extra=[prev_apply])
            tap = ln_apply(pv(A_XN, D), pv(A_XL, D), tst)
            prev_apply = tap
            ttr = transpose_chunks(A_XT, 512, s * 128, A_XN, tap)
            ttr_blk += ttr
            if DBG and row0 == 0 and s == 0:
                r.dma("se", "dse", dd1_h.ap(), pv(A_XN, D), waits=[tap])
                r.dma("se", "dse", dd3_h.ap(), stt[:], waits=[tap])
                r.dma("se", "dse", dd2_h.ap(), pv(A_XT, 8192), waits=ttr)
        for (wh, dest, qscale) in ((wq_h, qd_h, 0.125), (wk_h, kd_h, 1.0)):
            for wv_ in range(2):
                dts = [None] * 16
                step_tok = [None] * 16
                for ic in range(16):
                    buf = A_WA if (ic % 2 == 0) else A_WB
                    dts[ic] = r.dma("sy", "dsy", pv(buf, 1024),
                                    bass.AP(wh, ic * 128 * D + wv_ * 1024,
                                            [[D, 128], [1, 1024]]),
                                    waits=[step_tok[ic - 2] if ic >= 2
                                           else prev15])
                    tok = None
                    for j in range(8):
                        w = [dts[ic]]
                        if ic == 0:
                            w += [bank_tok[j]] + ttr_blk
                        tok = r.op("te", "matmul", ps[j][:, 0:nt],
                                   pv(buf + j * 128, 128),
                                   pv(A_XT + ic * 512, nt),
                                   start=(ic == 0), stop=(ic == 15),
                                   waits=w, inc=(j == 7))
                    step_tok[ic] = tok
                prev15 = step_tok[15]
                for j in range(8):
                    ev = r.op("se", "activation", pv(A_KS + j * 512, nt),
                              ps[j][:, 0:nt], AF.Copy, scale=qscale,
                              waits=[step_tok[15], stage_tok[j]])
                    bank_tok[j] = ev
                    och = wv_ * 8 + j
                    stage_tok[j] = r.dma("se", "dse",
                                         bass.AP(dest, och * 128 * PAD + row0,
                                                 [[PAD, 128], [1, nt]]),
                                         pv(A_KS + j * 512, nt), waits=[ev])
        for wv_ in range(2):
            dts = [None] * 16
            step_tok = [None] * 16
            for ic in range(16):
                buf = A_WA if (ic % 2 == 0) else A_WB
                dts[ic] = r.dma("sy", "dsy", pv(buf, 1024),
                                bass.AP(wv_h, ic * 128 * D + wv_ * 1024,
                                        [[D, 128], [1, 1024]]),
                                waits=[step_tok[ic - 2] if ic >= 2 else prev15])
                tok = None
                for s in range(nsub):
                    for dp in range(2):
                        j = s * 2 + dp
                        w = [dts[ic]]
                        if ic == 0:
                            w += [bank_tok[j]]
                        tok = r.op("te", "matmul", ps[j][:, 0:512],
                                   pv(A_XT + ic * 512 + s * 128, 128),
                                   pv(buf + dp * 512, 512),
                                   start=(ic == 0), stop=(ic == 15),
                                   waits=w, inc=(s == nsub - 1 and dp == 1))
                step_tok[ic] = tok
            prev15 = step_tok[15]
            for s in range(nsub):
                vso = A_VS + (s % 2) * D
                evs = []
                for dp in range(2):
                    j = s * 2 + dp
                    ev = copy_evac("ve" if dp == 0 else "se",
                                   pv(vso + wv_ * 1024 + dp * 512, 512),
                                   ps[j][:, 0:512],
                                   [step_tok[15], vst[s % 2]])
                    bank_tok[j] = ev
                    evs.append(ev)
                vst[s % 2] = r.dma("se", "dse",
                                   bass.AP(vd_h,
                                           (row0 + s * 128) * D + wv_ * 1024,
                                           [[D, 128], [1, 1024]]),
                                   pv(vso + wv_ * 1024, 1024), waits=evs)
    r.barrier()

    war = []
    prev_texp = [None, None]
    for t in range(8):
        q0 = 128 + t * 256
        k0 = t * 256
        dq, dk, dv = [], [], []
        for ic in range(16):
            dq.append(r.dma("sy", "dsy", pv(A_QW + ic * 256, 256),
                            bass.AP(qd_h, ic * 128 * PAD + q0,
                                    [[PAD, 128], [1, 256]]), waits=war))
            dk.append(r.dma("sy", "dsy", pv(A_KW + ic * 384, 384),
                            bass.AP(kd_h, ic * 128 * PAD + k0,
                                    [[PAD, 128], [1, 384]]), waits=war))
        for sl in range(3):
            dv.append(r.dma("se", "dse", pv(A_VW + sl * D, D),
                            bass.AP(vd_h, (k0 + sl * 128) * D,
                                    [[D, 128], [1, D]]), waits=war))
        war = []
        ost = [None] * 16
        for qb in range(2):
            for hp in range(16):
                mofs = 0 if (t == 0 and qb == 0) else 256
                ptoks = []
                for par in range(2):
                    p0 = par * 64
                    bk = par
                    r.op("te", "matmul", ps[bk][:, 0:256], cIdb[:],
                         cMask[:, mofs:mofs + 256], start=True, stop=False,
                         waits=[bank_tok[bk], dq[hp], dk[hp]], inc=False)
                    tqk = r.op("te", "matmul", ps[bk][:, 0:256],
                               pf[p0:p0 + 64,
                                  A_QW + hp * 256 + qb * 128:
                                  A_QW + hp * 256 + qb * 128 + 128],
                               pf[p0:p0 + 64,
                                  A_KW + hp * 384 + qb * 128:
                                  A_KW + hp * 384 + qb * 128 + 256],
                               start=False, stop=True)
                    mc = 40 + par * 3
                    tmx = r.op("ve", "reduce_max", stt[:, mc:mc + 1],
                               ps[bk][:, 0:256], AX.X, negate=True,
                               waits=[tqk, prev_texp[par]])
                    poff = A_P0 if par == 0 else A_P1
                    texp = r.op("se", "activation", pv(poff, 256),
                                ps[bk][:, 0:256], AF.Exp,
                                bias=stt[:, mc:mc + 1], scale=1.0,
                                accum_out=stt[:, mc + 1:mc + 2], waits=[tmx])
                    prev_texp[par] = texp
                    bank_tok[bk] = texp
                    trc = r.op("ve", "reciprocal", stt[:, mc + 2:mc + 3],
                               stt[:, mc + 1:mc + 2], waits=[texp])
                    pnoff = A_PN0 if par == 0 else A_PN1
                    tpn = r.op("ve", "tensor_single_scalar", pv(pnoff, 256),
                               pv(poff, 256), stt[:, mc + 2:mc + 3], ALU.mult,
                               waits=[trc])
                    ptoff = A_PT0 if par == 0 else A_PT1
                    bpt = 2 + par
                    for kc in range(2):
                        ttp = r.op("te", "transpose",
                                   ps[bpt][:, kc * 128:kc * 128 + 128],
                                   pv(pnoff + kc * 128, 128), cId[:],
                                   waits=[tpn, bank_tok[bpt]])
                        tev = copy_evac("ve" if kc == 0 else "se",
                                        pv(ptoff + kc * 128, 128),
                                        ps[bpt][:, kc * 128:kc * 128 + 128],
                                        [ttp])
                        bank_tok[bpt] = tev
                        ptoks.append(tev)
                pvt = []
                for par in range(2):
                    h = 2 * hp + par
                    ptoff = A_PT0 if par == 0 else A_PT1
                    tok = None
                    for kc in range(2):
                        w = [ptoks[par * 2 + kc], dv[qb + kc]]
                        if par == 0 and kc == 0:
                            w.append(bank_tok[4])
                        tok = r.op("te", "matmul",
                                   ps[4][par * 64:par * 64 + 64, 0:128],
                                   pv(A_VW + (qb + kc) * D + h * 64, 64),
                                   pv(ptoff + kc * 128, 128),
                                   start=(kc == 0), stop=(kc == 1),
                                   tile_position=(0, par * 64),
                                   waits=w, inc=(kc == 1))
                    pvt.append(tok)
                ev = r.op("se", "activation",
                          pv(A_OS + hp * 256 + qb * 128, 128),
                          ps[4][:, 0:128], AF.Copy, waits=pvt + [ost[hp]])
                bank_tok[4] = ev
                if qb == 1:
                    ost[hp] = r.dma("se", "dse",
                                    bass.AP(od_h, hp * 128 * TOK + t * 256,
                                            [[TOK, 128], [1, 256]]),
                                    pv(A_OS + hp * 256, 256), waits=[ev])
        war = [r.now("te"), r.now("ve"), r.now("se")] + [x for x in ost if x]
    r.barrier()

    war = []
    prev_tln = None
    prev15w = None
    for t in range(4):
        dow = []
        for ic in range(16):
            dow.append(r.dma("sy", "dsy", pv(B_OW + ic * 512, 512),
                             bass.AP(od_h, ic * 128 * TOK + t * 512,
                                     [[TOK, 128], [1, 512]]), waits=war))
        war = []
        store_tok = None
        for wv_ in range(2):
            dts = [None] * 16
            step_tok = [None] * 16
            for ic in range(16):
                buf = B_WA if (ic % 2 == 0) else B_WB
                dts[ic] = r.dma("sy", "dsy", pv(buf, 2048),
                                bass.AP(wo_h, ic * 128 * D, [[D, 128], [1, D]]),
                                waits=[step_tok[ic - 2] if ic >= 2 else prev15w])
                tok = None
                for sp in range(2):
                    sub = wv_ * 2 + sp
                    for db in range(4):
                        j = sp * 4 + db
                        w = [dts[ic], dow[ic]]
                        if ic == 0:
                            w += [bank_tok[j]]
                        tok = r.op("te", "matmul", ps[j][:, 0:512],
                                   pv(B_OW + ic * 512 + sub * 128, 128),
                                   pv(buf + db * 512, 512),
                                   start=(ic == 0), stop=(ic == 15),
                                   waits=w, inc=(sp == 1 and db == 3))
                step_tok[ic] = tok
            prev15w = step_tok[15]
            for sp in range(2):
                sub = wv_ * 2 + sp
                evs = []
                for db in range(4):
                    j = sp * 4 + db
                    ev = r.op("se", "activation", pv(B_Y + sp * D + db * 512, 512),
                              ps[j][:, 0:512], AF.Copy,
                              waits=[step_tok[15], prev_tln])
                    bank_tok[j] = ev
                    evs.append(ev)
                tst = ln_stats(pv(B_Y + sp * D, D), B_JK, evs[0],
                               extra=evs + [prev_tln])
                tln = ln_apply(pv(B_JK, D), pv(B_Y + sp * D, D), tst,
                               extra=evs)
                prev_tln = tln
                txl = r.dma("se", "dse", pv(B_XL, D),
                            bass.AP(x_h, (128 + t * 512 + sub * 128) * D,
                                    [[D, 128], [1, D]]),
                            waits=[store_tok])
                tadd = r.op("ve", "tensor_add", pv(B_XL, D), pv(B_JK, D),
                            pv(B_XL, D), waits=[tln, txl])
                store_tok = r.dma("se", "dse",
                                  bass.AP(xm_h, (t * 512 + sub * 128) * D,
                                          [[D, 128], [1, D]]),
                                  pv(B_XL, D), waits=[tadd])
        war = [r.now("te"), store_tok]
    r.barrier()

    prev_apply = None
    prevd63 = None
    for t in range(8):
        ttr_blk = []
        ttr = []
        for s in range(2):
            tin = r.dma("se", "dse", pv(F_X2, D),
                        bass.AP(xm_h, (t * 256 + s * 128) * D,
                                [[D, 128], [1, D]]),
                        waits=ttr + [r.now("te"), r.now("ve")])
            tst = ln_stats(pv(F_X2, D), F_V, tin, extra=[prev_apply])
            tap = ln_apply(pv(F_Y, D), pv(F_X2, D), tst)
            prev_apply = tap
            ttr = transpose_chunks(F_OVL, 256, s * 128, F_Y, tap)
            ttr_blk += ttr
        s1t = []
        for c in range(16):
            b = c % 8
            tm = r.op("te", "matmul", ps[b][:, 0:256], cH[:],
                      pv(F_OVL + c * 256, 256), start=True, stop=True,
                      waits=[bank_tok[b]] + (ttr_blk if c == 0 else []))
            ev = copy_evac("ve" if c % 2 == 0 else "se",
                           pap(F_ZG1 + c * 8, [[128, 32], [1, 8]]),
                           psap(b, [[8, 32], [1, 8]]), [tm])
            bank_tok[b] = ev
            s1t.append(ev)
        f1t = []
        for g in range(32):
            ba = 2 + (g % 2)
            bb = 4 + (g % 2)
            zt = F_ZTA if (g % 2 == 0) else F_ZTB
            tt = r.op("te", "transpose", ps[ba][:, 0:128],
                      pv(F_ZG1 + g * 128, 128), cId[:],
                      waits=(s1t if g == 0 else []) + [bank_tok[ba]])
            tz = r.op("ve", "tensor_copy", pv(zt, 128), ps[ba][:, 0:128],
                      waits=[tt])
            bank_tok[ba] = tz
            tm = r.op("te", "matmul", ps[bb][:, 0:128], pv(zt, 128), cW16[:],
                      start=True, stop=True, waits=[tz, bank_tok[bb]])
            ev = r.op("se", "activation", pv(F_ZG1 + g * 128, 128),
                      ps[bb][:, 0:128], AF.Copy, waits=[tm])
            bank_tok[bb] = ev
            f1t.append(ev)
        tmx = r.op("ve", "reduce_max", pv(F_V, 256),
                   bass.AP(pf, F_ZG1, [[PFC, 128], [128, 32], [1, 8], [8, 16]]),
                   AX.X, apply_absolute_value=True, waits=f1t)
        mtt = []
        for b2 in range(2):
            tt = r.op("te", "transpose", ps[6][:, 0:128],
                      pv(F_V + b2 * 128, 128), cId[:],
                      waits=[tmx, bank_tok[6]])
            tr_ = r.op("ve", "reduce_max", stt[:, 50 + b2:51 + b2],
                       ps[6][:, 0:128], AX.X, waits=[tt])
            bank_tok[6] = tr_
            mtt.append(tr_)
        tme = r.op("ve", "tensor_scalar", stt[:, 52:54], stt[:, 50:52],
                   C2048, EPS, ALU.mult, ALU.max, waits=mtt)
        trc = r.op("ve", "reciprocal", stt[:, 54:56], stt[:, 52:54],
                   waits=[tme])
        tvc = r.op("ve", "tensor_single_scalar", stt[:, 56:58], stt[:, 54:56],
                   float(np.float32(127.0) * np.float32(C2048)), ALU.mult,
                   waits=[trc])
        trg0 = r.op("ve", "tensor_single_scalar", stt[:, 62:64],
                    stt[:, 52:54], -INV127, ALU.mult, waits=[tme])
        trg = r.op("ve", "tensor_single_scalar", stt[:, 58:60],
                   stt[:, 62:64], stt[:, 0:1], ALU.mult, waits=[trg0])
        tru = r.op("ve", "tensor_single_scalar", stt[:, 60:62],
                   stt[:, 62:64], stt[:, 8:9], ALU.mult, waits=[trg0])
        tsr = r.op("te", "transpose", ps[6][0:8, 0:128], stt[:, 56:64], cId[:],
                   waits=[tvc, trg, tru, bank_tok[6]])
        tsr = r.op("ve", "tensor_copy", srow[:], ps[6][0:8, 0:128], waits=[tsr])
        bank_tok[6] = tsr
        vbct = []
        for b2 in range(2):
            vbct.append(r.op("te", "matmul", ps[7][:, b2 * 128:b2 * 128 + 128],
                             cSel[:, b2 * 128:b2 * 128 + 128], srow[:],
                             start=True, stop=True,
                             waits=[tsr, bank_tok[7]]))
        tvb = r.op("ve", "tensor_copy", pv(F_VB1, 256), ps[7][:, 0:256],
                   waits=vbct)
        bank_tok[7] = tvb
        k1done = []
        for sl in range(2):
            vm = []
            for cl in range(8):
                c = sl * 8 + cl
                vm.append(r.op("ve", "tensor_tensor",
                               pap(F_V + cl * 256, [[8, 32], [1, 8]]),
                               pap(F_ZG1 + c * 8, [[128, 32], [1, 8]]),
                               pap(F_VB1, [[8, 32], [1, 8]]),
                               ALU.mult, waits=[tvb]))
            tc = r.op("se", "activation", i32t[:, 0:2048], pv(F_V, 2048),
                      AF.Copy, waits=vm)
            tr_ = r.op("ve", "tensor_sub", pv(F_V, 2048), pv(F_V, 2048),
                       i32t[:, 0:2048], waits=[tc])
            tt1 = r.op("ve", "scalar_tensor_tensor", pv(F_Y, 2048),
                       pv(F_V, 2048), -0.5, i32t[:, 0:2048],
                       ALU.is_le, ALU.subtract, waits=[tr_])
            tk = r.op("ve", "scalar_tensor_tensor",
                      bap(F_K1 + sl * 2048, [[256, 8], [8, 32], [1, 8]]),
                      pap(F_V, [[256, 8], [8, 32], [1, 8]]), 0.5,
                      pap(F_Y, [[256, 8], [8, 32], [1, 8]]),
                      ALU.is_ge, ALU.subtract, waits=[tt1])
            k1done.append(tk)
        rgbt = []
        for b2 in range(2):
            rgbt.append(r.op("te", "matmul", ps[7][:, b2 * 128:b2 * 128 + 128],
                             cSel[:, (2 + b2) * 128:(2 + b2) * 128 + 128],
                             srow[:], start=True, stop=True,
                             waits=[bank_tok[7]] + k1done))
        trgb = r.op("ve", "tensor_copy", pv(F_VB1, 256), ps[7][:, 0:256],
                    waits=rgbt)
        bank_tok[7] = trgb
        gu_done = []
        prevg15 = None
        prevu15 = None
        for wv_ in range(8):
            dtg = [None] * 16
            step_tok = [None] * 16
            for ic in range(16):
                bg = F_WG if (ic % 2 == 0) else F_WG2
                dtg[ic] = r.dma("sy", "dsy", pb[:, bg:bg + 1024],
                                bass.AP(tg_h, ic * 128 * DI + wv_ * 1024,
                                        [[DI, 128], [1, 1024]]),
                                waits=[step_tok[ic - 2] if ic >= 2 else prevg15])
                tok = None
                for j in range(8):
                    w = [dtg[ic], k1done[ic // 8]]
                    if ic == 0:
                        w += [bank_tok[j]]
                    tok = r.op("te", "matmul", ps[j][:, 0:256],
                               pb[:, bg + j * 128:bg + j * 128 + 128],
                               pb[:, F_K1 + ic * 256:F_K1 + ic * 256 + 256],
                               start=(ic == 0), stop=(ic == 15),
                               waits=w, inc=(j == 7))
                step_tok[ic] = tok
            prevg15 = step_tok[15]
            gev = []
            for j in range(8):
                oc = wv_ * 8 + j
                t1_ = r.op("ve", "tensor_tensor", pv(F_PS + oc * 256, 256),
                           ps[j][:, 0:256], pv(F_VB1, 256), ALU.mult,
                           waits=[step_tok[15], trgb])
                t2_ = r.op("se", "activation", pv(F_PS + oc * 256, 256),
                           pv(F_PS + oc * 256, 256), AF.Silu, waits=[t1_])
                bank_tok[j] = t1_
                gev.append(t2_)
            ustep = [None] * 16
            for ic in range(16):
                bu = F_WU if (ic % 2 == 0) else F_WU2
                dtu = r.dma("sy", "dsy", pb[:, bu:bu + 1024],
                            bass.AP(tu_h, ic * 128 * DI + wv_ * 1024,
                                    [[DI, 128], [1, 1024]]),
                            waits=[ustep[ic - 2] if ic >= 2 else prevu15])
                tok = None
                for j in range(8):
                    w = [dtu]
                    if ic == 0:
                        w += [bank_tok[j], gev[j]]
                    tok = r.op("te", "matmul", ps[j][:, 0:256],
                               pb[:, bu + j * 128:bu + j * 128 + 128],
                               pb[:, F_K1 + ic * 256:F_K1 + ic * 256 + 256],
                               start=(ic == 0), stop=(ic == 15),
                               waits=w, inc=(j == 7))
                ustep[ic] = tok
            prevu15 = ustep[15]
            for j in range(8):
                oc = wv_ * 8 + j
                tp = r.op("ve", "tensor_tensor", pv(F_PS + oc * 256, 256),
                          ps[j][:, 0:256], pv(F_PS + oc * 256, 256), ALU.mult,
                          waits=[ustep[15], gev[j]])
                bank_tok[j] = tp
                gu_done.append(tp)
        k2done = []
        for hf in range(2):
            s1t2 = []
            for c in range(64):
                b = c % 8
                w = [bank_tok[b]]
                if c == 0:
                    w += gu_done + k2done
                tm = r.op("te", "matmul", ps[b][:, 0:128], cH[:],
                          pv(F_PS + c * 256 + hf * 128, 128),
                          start=True, stop=True, waits=w)
                ev = copy_evac("ve" if c % 2 == 0 else "se",
                               pap(F_OVL + c * 2, [[128, 64], [1, 2]]),
                               psap(b, [[2, 64], [1, 2]]), [tm])
                bank_tok[b] = ev
                s1t2.append(ev)
            f2t = []
            for g in range(64):
                ba = 2 + (g % 2)
                bb = 4 + (g % 2)
                zt = F_ZTA if (g % 2 == 0) else F_ZTB
                tt = r.op("te", "transpose", ps[ba][:, 0:128],
                          pv(F_OVL + g * 128, 128), cId[:],
                          waits=(s1t2 if g == 0 else []) + [bank_tok[ba]])
                tz = r.op("ve", "tensor_copy", pv(zt, 128), ps[ba][:, 0:128],
                          waits=[tt])
                bank_tok[ba] = tz
                tm = r.op("te", "matmul", ps[bb][:, 0:128], pv(zt, 128),
                          cW64[:], start=True, stop=True,
                          waits=[tz, bank_tok[bb]])
                ev = r.op("se", "activation", pv(F_OVL + g * 128, 128),
                          ps[bb][:, 0:128], AF.Copy, waits=[tm])
                bank_tok[bb] = ev
                f2t.append(ev)
            tmx = r.op("ve", "reduce_max", pv(F_V, 128),
                       bass.AP(pf, F_OVL,
                               [[PFC, 128], [128, 64], [1, 2], [2, 64]]),
                       AX.X, apply_absolute_value=True, waits=f2t)
            tt = r.op("te", "transpose", ps[6][:, 0:128], pv(F_V, 128), cId[:],
                      waits=[tmx, bank_tok[6]])
            tmt = r.op("ve", "reduce_max", stt[:, 32:33], ps[6][:, 0:128],
                       AX.X, waits=[tt])
            bank_tok[6] = tmt
            ttp = r.op("ve", "tensor_mul", stt[:, 33:34], stt[:, 32:33],
                       stt[:, 60 + hf:61 + hf], waits=[tmt])
            tab = r.op("ve", "tensor_single_scalar", stt[:, 33:34],
                       stt[:, 33:34], -1.0, ALU.mult, waits=[ttp])
            tme = r.op("ve", "tensor_scalar", stt[:, 36 + hf:37 + hf],
                       stt[:, 33:34], C8192, EPS, ALU.mult, ALU.max,
                       waits=[tab])
            trc = r.op("ve", "reciprocal", stt[:, 34:35],
                       stt[:, 36 + hf:37 + hf], waits=[tme])
            tk2 = r.op("ve", "tensor_single_scalar", stt[:, 34:35],
                       stt[:, 34:35],
                       float(np.float32(127.0) * np.float32(C8192)), ALU.mult,
                       waits=[trc])
            tvc2 = r.op("ve", "tensor_mul", stt[:, 35:36], stt[:, 34:35],
                        stt[:, 60 + hf:61 + hf], waits=[tk2])
            td1 = r.op("ve", "tensor_single_scalar", stt[:, 38 + hf:39 + hf],
                       stt[:, 36 + hf:37 + hf], -INV127, ALU.mult, waits=[tme])
            td2 = r.op("ve", "tensor_single_scalar", stt[:, 38 + hf:39 + hf],
                       stt[:, 38 + hf:39 + hf], stt[:, 16:17], ALU.mult,
                       waits=[td1])
            tsr = r.op("te", "transpose", ps[6][0:8, 0:128], stt[:, 32:40],
                       cId[:], waits=[tvc2, td2, bank_tok[6]])
            tsr = r.op("ve", "tensor_copy", srow[:], ps[6][0:8, 0:128],
                       waits=[tsr])
            bank_tok[6] = tsr
            tb = r.op("te", "matmul", ps[7][:, 0:128],
                      cSel[:, 3 * 128:3 * 128 + 128], srow[:],
                      start=True, stop=True, waits=[tsr, bank_tok[7]])
            tvb2 = r.op("ve", "tensor_copy", pv(F_VB2, 128), ps[7][:, 0:128],
                        waits=[tb])
            bank_tok[7] = tvb2
            for sl in range(4):
                vm = []
                for cl in range(16):
                    c = sl * 16 + cl
                    vm.append(r.op("ve", "tensor_tensor",
                                   pap(F_V + cl * 128, [[2, 64], [1, 2]]),
                                   pap(F_OVL + c * 2, [[128, 64], [1, 2]]),
                                   pap(F_VB2, [[2, 64], [1, 2]]),
                                   ALU.mult, waits=[tvb2]))
                tc = r.op("se", "activation", i32t[:, 0:2048], pv(F_V, 2048),
                          AF.Copy, waits=vm)
                tr_ = r.op("ve", "tensor_sub", pv(F_V, 2048), pv(F_V, 2048),
                           i32t[:, 0:2048], waits=[tc])
                tt1 = r.op("ve", "scalar_tensor_tensor", pv(F_Y, 2048),
                           pv(F_V, 2048), -0.5, i32t[:, 0:2048],
                           ALU.is_le, ALU.subtract, waits=[tr_])
                tk = r.op("ve", "scalar_tensor_tensor",
                          bap(F_K2 + sl * 16 * 256 + hf * 128,
                              [[256, 16], [2, 64], [1, 2]]),
                          pap(F_V, [[128, 16], [2, 64], [1, 2]]), 0.5,
                          pap(F_Y, [[128, 16], [2, 64], [1, 2]]),
                          ALU.is_ge, ALU.subtract, waits=[tt1])
                k2done.append(tk)
        dtd = [None] * 64
        step_tok = [None] * 64
        for ic in range(64):
            bd = F_TD if (ic % 2 == 0) else F_TD2
            dtd[ic] = r.dma("sy", "dsy", pb[:, bd:bd + 2048],
                            bass.AP(td_h, ic * 128 * D, [[D, 128], [1, D]]),
                            waits=[step_tok[ic - 2] if ic >= 2 else prevd63])
            tok = None
            for sub in range(2):
                for db in range(4):
                    j = sub * 4 + db
                    w = [dtd[ic]]
                    if ic == 0:
                        w += [bank_tok[j]] + k2done
                    tok = r.op("te", "matmul", ps[j][:, 0:512],
                               pb[:, F_K2 + ic * 256 + sub * 128:
                                  F_K2 + ic * 256 + sub * 128 + 128],
                               pb[:, bd + db * 512:bd + db * 512 + 512],
                               start=(ic == 0), stop=(ic == 63),
                               waits=w, inc=(sub == 1 and db == 3))
            step_tok[ic] = tok
        prevd63 = step_tok[63]
        prev_store = None
        for sub in range(2):
            evs = []
            for db in range(4):
                j = sub * 4 + db
                ev = r.op("se", "activation", pv(F_Y + db * 512, 512),
                          ps[j][:, 0:512], AF.Copy,
                          scale=stt[:, 38 + sub:39 + sub],
                          waits=[step_tok[63], prev_apply])
                bank_tok[j] = ev
                evs.append(ev)
            tst = ln_stats(pv(F_Y, D), F_V, evs[0], extra=evs + [prev_apply])
            tln = ln_apply(pv(F_V, D), pv(F_Y, D), tst, extra=evs)
            prev_apply = tln
            txl = r.dma("se", "dse", pv(F_X2, D),
                        bass.AP(xm_h, (t * 256 + sub * 128) * D,
                                [[D, 128], [1, D]]),
                        waits=[prev_store, r.now("ve")])
            tadd = r.op("ve", "tensor_add", pv(F_X2, D), pv(F_V, D),
                        pv(F_X2, D), waits=[tln, txl])
            prev_store = r.dma("se", "dse",
                               bass.AP(y_h, (t * 256 + sub * 128) * D,
                                       [[D, 128], [1, D]]),
                               pv(F_X2, D), waits=[tadd])
    r.barrier()

    sem_names = ["ste", "sve", "sse", "sge", "ssy", "dsy", "dse", "dge"]
    from contextlib import ExitStack
    es = ExitStack()
    with es:
        sems = {n: es.enter_context(nc.semaphore(n)) for n in sem_names}
        with nc.Block() as blk:
            def mk(name):
                serial = name in ("ve", "se", "ge")
                own = ESEM[name]

                def f(eng):
                    cnt = 0
                    for meth, args, kw, ws, inc in r.ops[name]:
                        for (s, v) in ws:
                            eng.wait_ge(sems[s], v)
                        if meth is None:
                            continue
                        if serial and cnt > 0:
                            eng.wait_ge(sems[own], cnt)
                        inst = getattr(eng, meth)(*args, **kw)
                        if inc is not None:
                            inst.then_inc(sems[inc[0]], inc[1])
                            if inc[0] == own:
                                cnt += 1
                return f
            blk.tensor(mk("te"))
            blk.vector(mk("ve"))
            blk.scalar(mk("se"))
            blk.gpsimd(mk("ge"))
            blk.sync(mk("sy"))
    return nc


def _hadamard(n):
    h = np.array([[1.0]], dtype=np.float64)
    while h.shape[0] < n:
        h = np.block([[h, h], [h, -h]])
    return h


def _consts():
    H128 = _hadamard(128).astype(np.float32)
    W16 = np.kron(_hadamard(16), np.eye(8)).astype(np.float32)
    W64 = np.kron(_hadamard(64), np.eye(2)).astype(np.float32)
    idn = np.eye(128, dtype=np.float32)
    idb = np.eye(128).astype(ml_dtypes.bfloat16)
    rr = np.arange(128)[:, None]
    cc = np.arange(256)[None, :]
    m1 = np.where((cc >= rr) & (cc <= rr + 128), 0.0, NEGM).astype(np.float32)
    m0 = np.where(cc < 128, NEGM, m1).astype(np.float32)
    sel = np.zeros((8, 1024), dtype=np.float32)
    for k in range(7):
        sel[k, k * 128:(k + 1) * 128] = 1.0
    sel[0, 896:1024] = 1.0
    return H128, W16, W64, idn, idb, m0, m1, sel


def kernel(L0_attn_x, L0_attn_Wq, L0_attn_Wk, L0_attn_Wv, L0_attn_Wo,
           L0_ffn_w_gate, L0_ffn_w_up, L0_ffn_w_down):
    x = np.ascontiguousarray(np.asarray(L0_attn_x, dtype=np.float32)
                             .reshape(2 * 8192, D))
    wqT = np.ascontiguousarray(np.asarray(L0_attn_Wq, np.float32).T)
    wkT = np.ascontiguousarray(np.asarray(L0_attn_Wk, np.float32).T)
    wvT = np.ascontiguousarray(np.asarray(L0_attn_Wv, np.float32).T)
    woT = np.ascontiguousarray(np.asarray(L0_attn_Wo, np.float32).T)
    wgT = np.ascontiguousarray(np.asarray(L0_ffn_w_gate, np.float32).T)
    wuT = np.ascontiguousarray(np.asarray(L0_ffn_w_up, np.float32).T)
    wdT = np.ascontiguousarray(np.asarray(L0_ffn_w_down, np.float32).T)
    H128, W16, W64, idn, idb, m0, m1, sel = _consts()
    mrest = m1.astype(ml_dtypes.bfloat16)

    in_maps = []
    for c in range(8):
        r0 = c * TOK
        if c % 4 == 0:
            halo = np.zeros((128, D), dtype=np.float32)
            mfirst = m0.astype(ml_dtypes.bfloat16)
        else:
            halo = x[r0 - 128:r0]
            mfirst = mrest
        xsh = np.ascontiguousarray(np.concatenate([halo, x[r0:r0 + TOK]], 0))
        cmask = np.ascontiguousarray(np.concatenate([mfirst, mrest], 1))
        in_maps.append(dict(
            x=xsh, wq=wqT, wk=wkT, wv=wvT, wo=woT, wg=wgT, wu=wuT, wd=wdT,
            ch128=H128, cw16=W16, cw64=W64, cid=idn, cidb=idb,
            cmask=cmask, csel=sel))
    nc = build()
    res = run_bass_kernel_spmd(nc, in_maps, core_ids=list(range(8)))
    y = np.concatenate([np.asarray(rr_["y"]) for rr_ in res.results], axis=0)
    return np.ascontiguousarray(y.reshape(2, 8192, D).astype(np.float32))


if __name__ == "__main__":
    build()
    print("build ok")
